# version 17
# baseline (speedup 1.0000x reference)
import math
import numpy as np
import ml_dtypes

import concourse.bass as bass
import concourse.mybir as mybir
from concourse.bass_utils import run_bass_kernel_spmd
from bass_rust import AP

F32 = mybir.dt.float32
BF16 = mybir.dt.bfloat16
AF = mybir.ActivationFunctionType
OP = mybir.AluOpType

DEBUG = False
MAGIC = 12582912.0
CSTAB = 32.0
EPS = 1e-5

NH = 16
DK = 64
DML = 1024
SQ = 4096
NQ = 2048
NPAIR = 8
NCH = 32
NT = 4
KB = 1024



def _bucket(rel):
    rel = np.asarray(rel, dtype=np.int64)
    nb = 16
    rb = (rel > 0).astype(np.int64) * nb
    ar = np.abs(rel)
    max_exact = 8
    is_small = ar < max_exact
    arf = np.maximum(ar, 1).astype(np.float32)
    rl = max_exact + (
        np.log(arf / np.float32(max_exact))
        / np.float32(math.log(128.0 / max_exact))
        * (nb - max_exact)
    ).astype(np.int64)
    rl = np.minimum(rl, nb - 1)
    return rb + np.where(is_small, ar, rl)


def _win_list(t):
    out = []
    for w in range(6):
        c = 4 * t + w - 1
        if 0 <= c <= 31:
            if t == 3 and w == 5:
                out.append((c, 1, 511))
            else:
                out.append((c, 0, 128 * w + 511))
    if t == 0:
        out.append((31, 2, 511))
    return out


_WINSET = set()
for _t in range(NT):
    for _c, _s, _o in _win_list(_t):
        _WINSET.add((_t, _c))

OFF_B0, LEN_B0 = 0, 1280
OFF_X, LEN_X = 1280, 640
OFF_Y, LEN_Y = 1920, 640
NVCOL = 2560


def _host_geometry(parity):
    oh = np.zeros((32, NVCOL), np.float32)
    m = np.arange(LEN_B0)
    oh[_bucket(m - 639), OFF_B0 + m] = 1.0
    m = np.arange(LEN_X)
    if parity == 0:
        oh[_bucket(m + 1), OFF_X + m] = 1.0
    else:
        oh[15, OFF_X + m] = 1.0
    m = np.arange(LEN_Y)
    if parity == 0:
        oh[_bucket(m + 3457), OFF_Y + m] = 1.0
    else:
        oh[_bucket(m - 639), OFF_Y + m] = 1.0

    sel_lo = np.zeros((NT, NCH), np.float32)
    sel_hi = np.zeros((NT, NCH), np.float32)
    for t in range(NT):
        for c in range(NCH):
            if (t, c) in _WINSET:
                continue
            dmin = 128 * c - 512 * t - 511
            dmax = 128 * c - 512 * t + 127
            ds = np.arange(dmin, dmax + 1)
            if parity == 1 and c >= 16:
                ds = ds - 4096
            bs = np.unique(_bucket(ds))
            assert bs.size == 1 and bs[0] in (15, 31), (parity, t, c, bs)
            if bs[0] == 15:
                sel_lo[t, c] = 1.0
            else:
                sel_hi[t, c] = 1.0
    sl = np.broadcast_to(sel_lo.reshape(1, -1), (128, 128)).copy()
    sh = np.broadcast_to(sel_hi.reshape(1, -1), (128, 128)).copy()
    return oh, sl, sh




class Sched:
    ENGS = ("sync", "gpsimd", "tensor", "vector", "scalar")

    def __init__(self, sems):
        self.ops = {e: [] for e in self.ENGS}
        self.sems = sems
        self.c = {k: 0 for k in sems}

    def emit(self, eng, fn):
        self.ops[eng].append(fn)

    def wait(self, eng, sem, val):
        if val <= 0:
            return
        s = self.sems[sem]
        self.ops[eng].append(lambda e, s=s, v=val: e.wait_ge(s, v))

    def inc(self, sem, n=1):
        self.c[sem] += n
        return self.c[sem]



def build_nc():
    nc = bass.Bass("TRN2", target_bir_lowering=False)

    def din(name, shape, dt=F32):
        return nc.dram_tensor(name, shape, dt, kind="ExternalInput").ap()

    x_d = din("x", [SQ, DML])
    wq_d = din("wq", [DML, DML])
    wk_d = din("wk", [DML, DML])
    wv_d = din("wv", [DML, DML])
    wo_d = din("wo", [DML, DML])
    rhs_d = din("rhs", [DML, DML])
    tbl_d = din("tbl", [32, NH])
    oh_d = din("ohall", [32, NVCOL])
    sel_lo_d = din("sel_lo", [128, 128])
    sel_hi_d = din("sel_hi", [128, 128])
    iden_d = din("iden", [128, 128])
    onescol_d = din("onescol", [128, 64])
    onesrow_d = din("onesrow", [128, 128])

    out_d = nc.dram_tensor("out", [NQ, DML], F32, kind="ExternalOutput").ap()
    ik = "ExternalOutput" if DEBUG else "Internal"
    vecd_d = nc.dram_tensor("vecd", [NH, NVCOL], F32, kind=ik).ap()
    ktd_d = nc.dram_tensor("ktd", [NPAIR, 128, SQ], BF16, kind=ik).ap()
    qtd_d = nc.dram_tensor("qtd", [NPAIR, 128, NQ], BF16, kind=ik).ap()
    if DEBUG:
        dbg_ot = nc.dram_tensor("d_ot", [128, 16 * KB], BF16, kind="ExternalOutput").ap()
        dbg_x2 = nc.dram_tensor("d_x2", [128, 16 * KB], F32, kind="ExternalOutput").ap()
        dbg_bias = nc.dram_tensor("d_bias", [128, 2 * KB], F32, kind="ExternalOutput").ap()
        dbg_v = nc.dram_tensor("d_v", [128, 32 * KB], BF16, kind="ExternalOutput").ap()
        dbg_qr = nc.dram_tensor("d_qr", [128, 8 * KB], BF16, kind="ExternalOutput").ap()
        dbg_lt0 = nc.dram_tensor("d_lt0", [128, 4096], BF16, kind="ExternalOutput").ap()
        dbg_kq = nc.dram_tensor("d_kq", [128, NQ], BF16, kind="ExternalOutput").ap()

    sbase = int(nc.sbuf_base)
    arena_bytes = 229344 - sbase - 512
    arena = nc.sbuf_tensor("arena", [128, arena_bytes], mybir.dt.uint8)
    arena_h = arena.__enter__()
    BASE = (sbase + 63) // 64 * 64

    def sb(name, shape, dt, off):
        return nc.alloc_sbuf_tensor_at(name, shape, dt, offset=BASE + off)

    V_sb = sb("V_sb", [128, 32 * KB], BF16, 0x00000)
    OT = sb("OT", [128, 16 * KB], BF16, 0x10000)
    Wo_bf = sb("Wo_bf", [128, 8 * KB], BF16, 0x18000)
    biastc = sb("biastc", [128, 2 * KB], F32, 0x1C000)
    iden_bf = sb("iden_bf", [128, 128], BF16, 0x1E000)
    iden_f = sb("iden_f", [128, 128], F32, 0x1E100)
    onescol_bf = sb("onescol_bf", [128, 64], BF16, 0x1E300)
    onesrow_bf = sb("onesrow_bf", [128, 128], BF16, 0x1E380)
    onesrow_f = sb("onesrow_f", [128, 128], F32, 0x1E480)
    sel_lo_sb = sb("sel_lo_sb", [128, 128], F32, 0x1E680)
    sel_hi_sb = sb("sel_hi_sb", [128, 128], F32, 0x1E880)
    tbl_sb = sb("tbl_sb", [32, NH], F32, 0x1EA80)
    vecs = sb("vecs", [128, 384], F32, 0x1EB00)
    crow = sb("crow", [1, 32], F32, 0x1FC00)
    tmpA = sb("tmpA", [128, 128], F32, 0x1FE00)
    tmpB = sb("tmpB", [128, 128], F32, 0x20000)
    srow = sb("srow", [8, 128], F32, 0x20200)
    den_bf = sb("den_bf", [128, 512], BF16, 0x20400)
    rcp_f = sb("rcp_f", [128, 512], F32, 0x20800)
    BC_sb = sb("BC_sb", [128, 512], F32, 0x21000)
    x3_sb = sb("x3_sb", [128, KB], F32, 0x21400)
    kstage = [sb(f"kstage{i}", [128, 512], BF16, 0x22800 + i * 0x400) for i in range(2)]
    qstage = [sb(f"qstage{i}", [128, 512], BF16, 0x23000 + i * 0x400) for i in range(2)]
    ln2t = sb("ln2t", [128, KB], F32, 0x22400)
    PT_h = sb("PT_h", [128, KB], BF16, 0x23800)
    PT_g = sb("PT_g", [128, KB], BF16, 0x24000)
    Bz0 = [[sb(f"Bz0_{h}_{i}", [128, 1152], F32, 0x24800 + (h * 2 + i) * 0x1200)
            for i in range(2)] for h in range(2)]
    BzX = [[sb(f"BzX_{h}_{i}", [128, 512], F32, 0x29000 + (h * 2 + i) * 0x800)
            for i in range(2)] for h in range(2)]
    BzY = [[sb(f"BzY_{h}_{i}", [128, 512], F32, 0x2B000 + (h * 2 + i) * 0x800)
            for i in range(2)] for h in range(2)]
    KT_buf = [sb(f"KT_buf{i}", [128, SQ], BF16, 0x2D000 + i * 0x2000) for i in range(2)]
    QT_buf = [sb(f"QT_buf{i}", [128, NQ], BF16, 0x31000 + i * 0x1000) for i in range(2)]
    Wq_bf = sb("Wq_bf", [128, 8 * KB], BF16, 0x10000)
    Wk_bf = sb("Wk_bf", [128, 8 * KB], BF16, 0x14000)
    Wv_bf = sb("Wv_bf", [128, 8 * KB], BF16, 0x24800)
    ln1T = [sb(f"ln1T{i}", [128, 8 * 512], BF16, 0x29000 + i * 0x2000) for i in range(2)]
    xT_sb = [sb(f"xT{i}", [128, KB], F32, 0x2D000 + i * 0x1000) for i in range(2)]
    xm_sb = [sb(f"xm{i}", [128, KB], F32, 0x2F000 + i * 0x1000) for i in range(2)]
    ln1_bf = [sb(f"ln1bf{i}", [128, KB], BF16, 0x31000 + i * 0x800) for i in range(2)]
    sqb1 = sb("sqb1", [128, KB], F32, 0x32000)
    W32 = sb("W32", [128, 8 * KB], F32, 0x00000)
    ohall_sb = sb("ohall_sb", [32, NVCOL], F32, 0x08000)
    vecall_sb = sb("vecall_sb", [NH, NVCOL], F32, 0x0B000)
    x2 = sb("x2", [128, 16 * KB], F32, 0x00000)
    rhsT_all = sb("rhsT_all", [128, 8 * KB], F32, 0x00000)
    xm3buf = [sb(f"xm3_{i}", [128, KB], F32, 0x10000 + i * 0x1000) for i in range(3)]
    ffnbuf = [sb(f"ffnb_{i}", [128, KB], F32, 0x18000 + i * 0x1000) for i in range(3)]
    RS_sb = sb("RS_sb", [128, KB], F32, 0x1C000)
    sqb3 = sb("sqb3", [128, KB], F32, 0x1D000)
    ffn_t = sb("ffn_t", [128, KB], F32, 0x20400)
    srow2 = sb("srow2", [1, KB], F32, 0x25400)
    qlhsT = sb("qlhsT", [128, 16 * KB], BF16, 0x27400)
    rhs32 = sb("rhs32", [128, 8 * KB], F32, 0x27400)
    outst = [sb(f"outst{i}", [128, KB], F32, 0x25400 + i * 0x1000) for i in range(2)]
    ln3t = sb("ln3t", [128, KB], F32, 0x23400)
    qlhsq = sb("qlhsq", [128, KB], BF16, 0x24400)
    qrhsT_jd = sb("qrhsT_jd", [128, KB], BF16, 0x24C00)
    qrhs_bf = sb("qrhs_bf", [128, 8 * KB], BF16, 0x2F400)

    psum_stack = []
    def ps(name, shape):
        g = nc.psum_tensor(name, shape, F32)
        psum_stack.append(g)
        return g.__enter__()

    PSA = ps("PSA", [128, 1024])
    PSB = ps("PSB", [128, 1024])
    ACC = ps("ACC", [128, 512])
    DEN = ps("DEN", [128, 512])
    BCP = ps("BCP", [128, 512])
    PX = ps("PX", [128, 512])

    def bf16view(psap):
        return psap[:].bitcast(BF16)

    sem_stack = []
    sems = {}
    for name in ("ld_c", "ld_x", "ld_k", "ld_3", "st", "mm", "act", "dve"):
        g = nc.semaphore(name)
        sem_stack.append(g)
        sems[name] = g.__enter__()

    S = Sched(sems)

    def t_mm(out, lhsT, rhs, start, stop, inc=False):
        v = S.inc("mm") if inc else None
        def f(e, out=out, l=lhsT, r=rhs, st=start, sp=stop, v=v):
            i = e.matmul(out, l, r, start=st, stop=sp)
            if v is not None:
                i.then_inc(sems["mm"], 1)
        S.emit("tensor", f)
        return S.c["mm"]

    def t_tr(out, in_, iden, inc=False):
        v = S.inc("mm") if inc else None
        def f(e, out=out, i_=in_, idn=iden, v=v):
            i = e.transpose(out, i_, idn)
            if v is not None:
                i.then_inc(sems["mm"], 1)
        S.emit("tensor", f)
        return S.c["mm"]

    def v_op(meth, *args, inc=False, **kw):
        v = S.inc("dve") if inc else None
        def f(e, m=meth, a=args, kw=kw, v=v):
            i = getattr(e, m)(*a, **kw)
            if v is not None:
                i.then_inc(sems["dve"], 1)
        S.emit("vector", f)
        return S.c["dve"]

    def a_op(meth, *args, inc=False, **kw):
        v = S.inc("act") if inc else None
        def f(e, m=meth, a=args, kw=kw, v=v):
            i = getattr(e, m)(*a, **kw)
            if v is not None:
                i.then_inc(sems["act"], 1)
        S.emit("scalar", f)
        return S.c["act"]

    def vpad(n=4):
        for _ in range(n):
            v_op("tensor_scalar", vecs[:, 380:381], vecs[:, 381:382], 1.0, None,
                 op0=OP.mult)

    def apad(n=4):
        for _ in range(n):
            a_op("activation", vecs[:, 378:379], vecs[:, 379:380], AF.Identity,
                 bias=0.0, scale=1.0)

    def dma(queue, sem, out, in_, slow=False):
        S.inc(sem, 16)
        def f(e, o=out, i=in_, s=sems[sem], slow=slow):
            kw = {"allow_slow_non_contiguous": True} if slow else {}
            e.dma_start(out=o, in_=i, **kw).then_inc(s, 16)
        S.emit(queue, f)
        return S.c[sem]

    vc = lambda c: vecs[:, c:c + 1]

    def ln_tail(mc, qc, m2c, varc, vepsc, rc, n):
        vpad(4)
        v_op("tensor_tensor", out=vecs[:, m2c:m2c + n], in0=vecs[:, mc:mc + n],
             in1=vecs[:, mc:mc + n], op=OP.mult)
        vpad(4)
        v_op("tensor_tensor", out=vecs[:, varc:varc + n], in0=vecs[:, qc:qc + n],
             in1=vecs[:, m2c:m2c + n], op=OP.subtract)
        vpad(4)
        v_op("tensor_scalar", vecs[:, vepsc:vepsc + n], vecs[:, varc:varc + n],
             EPS, None, op0=OP.add)
        vpad(4, )
        m1 = v_op("nop", inc=True)
        S.wait("scalar", "dve", m1)
        apad(8)
        a_op("activation", vecs[:, m2c:m2c + n], vecs[:, vepsc:vepsc + n], AF.Ln)
        apad(1)
        a_op("activation", vecs[:, rc:rc + n], vecs[:, m2c:m2c + n], AF.Exp,
             bias=0.0, scale=-0.5)
        apad(2)
        return a_op("nop", inc=True)

    def ln_accums(src_ap, mcol, qcol, sq):
        S.wait("scalar", "dve", S.c["dve"])
        sq_a = a_op("activation", sq[:], src_ap, AF.Square, inc=True)
        S.wait("vector", "act", sq_a)
        v_op("tensor_scalar", sq[:], sq[:], 1.0 / 1024.0, None, op0=OP.mult,
             op1=OP.add, accum_out=vc(qcol))
        v_op("tensor_scalar", sq[:], src_ap, 1.0 / 1024.0, None, op0=OP.mult,
             op1=OP.add, accum_out=vc(mcol), inc=True)
        return S.c["dve"]

    dma("sync", "ld_c", iden_f[:], iden_d)
    dma("sync", "ld_c", tmpA[:, 0:64], onescol_d)
    dma("sync", "ld_c", onesrow_f[:], onesrow_d)
    dma("sync", "ld_c", sel_lo_sb[:], sel_lo_d)
    dma("sync", "ld_c", sel_hi_sb[:], sel_hi_d)
    dma("sync", "ld_c", tbl_sb[:], tbl_d)
    ld0 = dma("sync", "ld_c", ohall_sb[:], oh_d)

    S.wait("vector", "ld_c", ld0)
    v_op("tensor_copy", iden_bf[:], iden_f[:])
    v_op("tensor_copy", onescol_bf[:], tmpA[:, 0:64])
    v_op("tensor_copy", onesrow_bf[:], onesrow_f[:], inc=True)

    wcast = {}
    for nm, d_ap, dst in (("wq", wq_d, Wq_bf), ("wk", wk_d, Wk_bf),
                          ("wv", wv_d, Wv_bf), ("wo", wo_d, Wo_bf)):
        src = AP(d_ap.tensor, 0, [[1024, 128], [131072, 8], [1, 1024]])
        if wcast:
            S.wait("sync", "dve", S.c["dve"])
        lw = dma("sync", "ld_c", W32[:], src)
        S.wait("vector", "ld_c", lw)
        wcast[nm] = ("dve", v_op("tensor_copy", dst[:], W32[:], inc=True))

    S.wait("tensor", "ld_c", ld0)
    for s5 in range(5):
        if s5:
            S.wait("tensor", "dve", S.c["dve"])
        t_mm(PX[0:16, 0:512], tbl_sb[:, 0:NH], ohall_sb[:, s5 * 512:(s5 + 1) * 512],
             True, True, inc=True)
        S.wait("vector", "mm", S.c["mm"])
        v_op("tensor_copy", vecall_sb[:, s5 * 512:(s5 + 1) * 512], PX[0:16, 0:512],
             inc=True)
    vec_done = S.c["dve"]
    S.wait("gpsimd", "dve", vec_done)
    st_vec = dma("gpsimd", "st", vecd_d, vecall_sb[:])
    S.wait("sync", "dve", vec_done)
    va = vecall_sb[:]
    crow_src = AP(va.tensor, va.offset + OFF_B0, [[NVCOL, 16], [1278, 2]])
    ldc2 = dma("sync", "ld_c", crow[:], crow_src, slow=True)
    S.wait("tensor", "ld_c", ldc2)
    bcc_mm = t_mm(DEN[0:128, 0:32], onesrow_f[0:1, 0:128], crow[0:1, 0:32],
                  True, True, inc=True)
    S.wait("vector", "mm", bcc_mm)
    vpad(12)
    for h in range(NH):
        v_op("tensor_scalar", tmpA[:], sel_lo_sb[:], DEN[:, 2 * h:2 * h + 1],
             -CSTAB, op0=OP.mult, op1=OP.add)
        v_op("tensor_scalar", tmpB[:], sel_hi_sb[:], DEN[:, 2 * h + 1:2 * h + 2],
             None, op0=OP.mult)
        vpad(2)
        v_op("tensor_tensor", out=biastc[:, h * 128:(h + 1) * 128], in0=tmpA[:],
             in1=tmpB[:], op=OP.add, inc=True)
    biastc_done = S.c["dve"]
    if DEBUG:
        S.wait("gpsimd", "dve", biastc_done)
        dma("gpsimd", "st", dbg_bias, biastc[:])

    xa_ld = {}
    statsA = {}
    for i in range(32):
        S.wait("sync", "dve", statsA.get(i - 2, 0))
        xa_ld[i] = dma("sync", "ld_x", xT_sb[i % 2][:], x_d[i * 128:(i + 1) * 128, :])
        S.wait("scalar", "ld_x", xa_ld[i])
        statsA[i] = ln_accums(xT_sb[i % 2][:], i, 32 + i, sqb1)
    vpad(4)
    v_op("tensor_scalar", vecs[:, 160:192], vecs[:, 0:32], -1.0, None,
         op0=OP.mult, inc=True)
    ln1_rstd_done = ln_tail(0, 32, 64, 96, 128, 96, 32)

    transp_done = {}
    pxcopy = {}
    xb_ld = {}
    xm_done = {}
    ap_done = {}
    ln1T_done = {}
    kevac = {}
    kstore = {}
    qevac = {}
    qstore = {}
    vevac = {}

    S.wait("scalar", "dve", S.c["dve"])
    p1a_dve = statsA[31]
    for s in range(8):
        for sub in range(4):
            i = 4 * s + sub
            S.wait("sync", "act", ap_done.get(i - 2, 0))
            if i < 2:
                S.wait("sync", "dve", p1a_dve)
            xb_ld[i] = dma("sync", "ld_x", xT_sb[i % 2][:],
                           x_d[i * 128:(i + 1) * 128, :])
            S.wait("scalar", "ld_x", xb_ld[i])
            S.wait("scalar", "mm", transp_done.get(i - 2, 0))
            a_op("activation", xm_sb[i % 2][:], xT_sb[i % 2][:],
                 AF.Identity, bias=vc(160 + i), scale=1.0)
            ap_done[i] = a_op("activation", ln1_bf[i % 2][:], xm_sb[i % 2][:],
                              AF.Identity, bias=0.0, scale=vc(96 + i), inc=True)
            S.wait("tensor", "act", ap_done[i])
            S.wait("tensor", "dve", pxcopy.get(i - 1, 0))
            pxv = bf16view(PX)
            for jf in range(8):
                t_tr(AP(pxv.tensor, pxv.offset + jf * 128, [[1024, 128], [1, 128]]),
                     ln1_bf[i % 2][:, jf * 128:(jf + 1) * 128], iden_bf[:],
                     inc=(jf == 7))
            transp_done[i] = S.c["mm"]
            S.wait("vector", "mm", transp_done[i])
            lt = ln1T[s % 2][:]
            dst = AP(lt.tensor, lt.offset + sub * 128, [[4096, 128], [512, 8], [1, 128]])
            srcp = AP(pxv.tensor, pxv.offset, [[1024, 128], [128, 8], [1, 128]])
            v_op("tensor_copy", dst, srcp, inc=True)
            pxcopy[i] = S.c["dve"]
        ln1T_done[s] = S.c["dve"]
        if DEBUG and s == 0:
            S.wait("gpsimd", "dve", ln1T_done[0])
            dma("gpsimd", "st", dbg_lt0, ln1T[0][:])

        for jp in range(NPAIR):
            S.wait("tensor", "dve", max(ln1T_done[s],
                                        kevac.get((s, jp - 2),
                                                  kevac.get((s - 1, jp + 6), 0))))
            S.wait("tensor", "dve", wcast["wk"][1])
            for jf in range(8):
                t_mm(PSA[:, (jp % 2) * 512:(jp % 2) * 512 + 512],
                     Wk_bf[:, jf * KB + jp * 128: jf * KB + jp * 128 + 128],
                     ln1T[s % 2][:, jf * 512:(jf + 1) * 512],
                     jf == 0, jf == 7, inc=(jf == 7))
            ktmm = S.c["mm"]
            S.wait("vector", "mm", ktmm)
            S.wait("vector", "st",
                   kstore.get((s, jp - 2), kstore.get((s - 1, jp + 6), 0)))
            kevac[(s, jp)] = v_op("tensor_copy", kstage[jp % 2][:],
                                  PSA[:, (jp % 2) * 512:(jp % 2) * 512 + 512],
                                  inc=True)
            S.wait("gpsimd", "dve", kevac[(s, jp)])
            dstk = AP(ktd_d.tensor, jp * 128 * SQ + s * 512, [[SQ, 128], [1, 512]])
            kstore[(s, jp)] = dma("gpsimd", "st", dstk, kstage[jp % 2][:])
        for sub in range(4):
            for dh in range(2):
                idx = sub * 2 + dh
                if s == 4 and idx == 0:
                    S.wait("vector", "st", st_vec)
                    S.wait("vector", "ld_c", S.c["ld_c"])
                S.wait("tensor", "dve",
                       vevac.get((s, idx - 2), vevac.get((s - 1, idx + 6), 0)))
                S.wait("tensor", "dve", wcast["wv"][1])
                for jf in range(8):
                    t_mm(PSB[:, (idx % 2) * 512:(idx % 2) * 512 + 512],
                         ln1T[s % 2][:, jf * 512 + sub * 128: jf * 512 + sub * 128 + 128],
                         Wv_bf[:, jf * KB + dh * 512: jf * KB + dh * 512 + 512],
                         jf == 0, jf == 7, inc=(jf == 7))
                vmm = S.c["mm"]
                S.wait("vector", "mm", vmm)
                vevac[(s, idx)] = v_op(
                    "tensor_copy",
                    V_sb[:, (4 * s + sub) * KB + dh * 512:(4 * s + sub) * KB + dh * 512 + 512],
                    PSB[:, (idx % 2) * 512:(idx % 2) * 512 + 512], inc=True)
        if s < 4:
            for jp in range(NPAIR):
                pb = ACC if jp % 2 == 0 else DEN
                S.wait("tensor", "dve",
                       max(qevac.get((s, jp - 2), qevac.get((s - 1, jp + 6), 0)),
                           biastc_done))
                S.wait("tensor", "dve", wcast["wq"][1])
                for jf in range(8):
                    t_mm(pb[:, 0:512],
                         Wq_bf[:, jf * KB + jp * 128: jf * KB + jp * 128 + 128],
                         ln1T[s % 2][:, jf * 512:(jf + 1) * 512],
                         jf == 0, jf == 7, inc=(jf == 7))
                qmm = S.c["mm"]
                S.wait("vector", "mm", qmm)
                S.wait("vector", "st",
                       qstore.get((s, jp - 2), qstore.get((s - 1, jp + 6), 0)))
                qevac[(s, jp)] = v_op("tensor_copy", qstage[jp % 2][:], pb[:, 0:512],
                                      inc=True)
                S.wait("gpsimd", "dve", qevac[(s, jp)])
                dstq = AP(qtd_d.tensor, jp * 128 * NQ + s * 512, [[NQ, 128], [1, 512]])
                qstore[(s, jp)] = dma("gpsimd", "st", dstq, qstage[jp % 2][:])

    p1_stores = S.c["st"]
    p1_mm = S.c["mm"]
    p1_dve = S.c["dve"]
    if DEBUG:
        S.wait("gpsimd", "dve", p1_dve)
        dma("gpsimd", "st", dbg_v, V_sb[:])
        S.wait("gpsimd", "st", p1_stores + 16)
        dma("gpsimd", "st", dbg_kq, AP(qtd_d.tensor, 0, [[NQ, 128], [1, NQ]]))


    k_ld = {}
    pair_end_mm = {-2: 0, -1: 0}

    def issue_pair_loads(jp):
        b2 = jp % 2
        if jp <= 1:
            S.wait("sync", "st", p1_stores)
            S.wait("sync", "mm", p1_mm)
        else:
            S.wait("sync", "mm", pair_end_mm[jp - 2])
        dma("sync", "ld_k", KT_buf[b2][:],
            AP(ktd_d.tensor, jp * 128 * SQ, [[SQ, 128], [1, SQ]]))
        dma("sync", "ld_k", QT_buf[b2][:],
            AP(qtd_d.tensor, jp * 128 * NQ, [[NQ, 128], [1, NQ]]))
        for hh in range(2):
            h = 2 * jp + hh
            S.wait("sync", "st", st_vec)
            dma("sync", "ld_k", Bz0[hh][b2][:],
                AP(vecd_d.tensor, h * NVCOL + OFF_B0, [[1, 128], [1, 1152]]))
            dma("sync", "ld_k", BzX[hh][b2][:],
                AP(vecd_d.tensor, h * NVCOL + OFF_X, [[1, 128], [1, 512]]))
            k_ld[jp] = dma("sync", "ld_k", BzY[hh][b2][:],
                           AP(vecd_d.tensor, h * NVCOL + OFF_Y, [[1, 128], [1, 512]]))

    issue_pair_loads(0)
    issue_pair_loads(1)

    exp_done = {}
    attnv_done = {}
    otevac = {}
    pend_epi = [None]

    def emit_attnv(jp, b2, t, cp):
        cgp = t * NCH + cp
        parp = cgp % 2
        S.wait("tensor", "act", exp_done[(jp, cgp)])
        if cp == 0:
            S.wait("tensor", "dve",
                   otevac.get((jp, t - 1), otevac.get((jp - 1, NT - 1), p1_dve)))
        st_ = cp == 0
        sp_ = cp == NCH - 1
        t_mm(ACC[0:64, :], V_sb[:, cp * KB + jp * 128: cp * KB + jp * 128 + 64],
             PT_h[:, parp * 512:parp * 512 + 512], st_, sp_)
        t_mm(ACC[64:128, :], V_sb[:, cp * KB + jp * 128 + 64: cp * KB + jp * 128 + 128],
             PT_g[:, parp * 512:parp * 512 + 512], st_, sp_)
        t_mm(DEN[0:64, :], onescol_bf[:], PT_h[:, parp * 512:parp * 512 + 512],
             st_, sp_)
        t_mm(DEN[64:128, :], onescol_bf[:], PT_g[:, parp * 512:parp * 512 + 512],
             st_, sp_, inc=True)
        attnv_done[(jp, cgp)] = S.c["mm"]

    for jp in range(NPAIR):
        b2 = jp % 2
        for t in range(NT):
            wins = dict((c, (srcid, off)) for c, srcid, off in _win_list(t))
            add_done = {}
            def emit_scores(c):
                cg = t * NCH + c
                par = cg % 2
                S.wait("tensor", "act",
                       exp_done.get((jp, cg - 2),
                                    exp_done.get((jp - 1, 4 * NCH + cg - 2), 0)))
                if cg == 0:
                    S.wait("tensor", "ld_k", k_ld[jp])
                    S.wait("tensor", "dve", p1_dve)
                t_mm(PSA[:, par * 512:par * 512 + 512],
                     KT_buf[b2][0:64, c * 128:(c + 1) * 128],
                     QT_buf[b2][0:64, t * 512:(t + 1) * 512], True, True)
                t_mm(PSB[:, par * 512:par * 512 + 512],
                     KT_buf[b2][64:128, c * 128:(c + 1) * 128],
                     QT_buf[b2][64:128, t * 512:(t + 1) * 512], True, True,
                     inc=True)
                return S.c["mm"]

            def emit_expadd(c, sc_mm):
                cg = t * NCH + c
                par = cg % 2
                if c in wins:
                    srcid, off = wins[c]
                    S.wait("vector", "mm", sc_mm)
                    for hh, PS in ((0, PSA), (1, PSB)):
                        bt = (Bz0, BzX, BzY)[srcid][hh][b2]
                        stride = 1152 if srcid == 0 else 512
                        a0 = bt[:]
                        rev = AP(a0.tensor, a0.offset + off, [[stride, 128], [-1, 512]])
                        v_op("tensor_tensor",
                             out=PS[:, par * 512:par * 512 + 512],
                             in0=PS[:, par * 512:par * 512 + 512],
                             in1=rev, op=OP.add, inc=(hh == 1))
                    add_done[c] = S.c["dve"]
                S.wait("scalar", "mm", sc_mm)
                if c in wins:
                    S.wait("scalar", "dve", add_done[c])
                S.wait("scalar", "mm",
                       attnv_done.get((jp, cg - 2),
                                      attnv_done.get((jp - 1, 4 * NCH + cg - 2), 0)))
                for hh, PS, PT in ((0, PSA, PT_h), (1, PSB, PT_g)):
                    h = 2 * jp + hh
                    col = h * 128 + t * 32 + c
                    a_op("activation", PT[:, par * 512:par * 512 + 512],
                         PS[:, par * 512:par * 512 + 512], AF.Exp,
                         bias=biastc[:, col:col + 1], scale=1.0, inc=(hh == 1))
                exp_done[(jp, cg)] = S.c["act"]

            for c in range(NCH + 2):
                if c < NCH:
                    sc0 = emit_scores(c)
                if c == 2 and pend_epi[0] is not None:
                    pend_epi[0]()
                    pend_epi[0] = None
                if c >= 2:
                    emit_attnv(jp, b2, t, c - 2)
                if c < NCH:
                    emit_expadd(c, sc0)
            S.wait("vector", "mm", attnv_done[(jp, t * NCH + NCH - 1)])
            v_op("reciprocal", rcp_f[:], DEN[:])
            v_op("tensor_copy", den_bf[0:1, :], rcp_f[0:1, :])
            v_op("tensor_copy", den_bf[64:65, :], rcp_f[64:65, :], inc=True)
            den_done = S.c["dve"]

            def mk_epi(jp=jp, t=t, den_done=den_done):
                def epi():
                    S.wait("tensor", "dve", den_done)
                    t_mm(BCP[0:64, :], onesrow_bf[0:1, 0:64], den_bf[0:1, :], True, True)
                    t_mm(BCP[64:128, :], onesrow_bf[64:65, 0:64], den_bf[64:65, :],
                         True, True, inc=True)
                    bcast_done = S.c["mm"]
                    S.wait("vector", "mm", bcast_done)
                    v_op("tensor_copy", BC_sb[:], BCP[:])
                    v_op("tensor_tensor",
                         out=OT[:, jp * NQ + t * 512: jp * NQ + t * 512 + 512],
                         in0=ACC[:], in1=BC_sb[:], op=OP.mult, inc=True)
                    otevac[(jp, t)] = S.c["dve"]
                return epi
            pend_epi[0] = mk_epi()
        pair_end_mm[jp] = S.c["mm"]
        if jp + 2 < NPAIR:
            issue_pair_loads(jp + 2)
    pend_epi[0]()
    pend_epi[0] = None

    p2_dve = S.c["dve"]
    S.wait("sync", "mm", pair_end_mm[NPAIR - 1])
    S.wait("sync", "dve", p2_dve)
    rsrc = AP(rhs_d.tensor, 0, [[1024, 128], [131072, 8], [1, 1024]])
    ld_rhs = dma("sync", "ld_3", rhs32[:], rsrc)

    pxv = bf16view(PX)
    rT_cp = {}
    for jd in range(8):
        S.wait("tensor", "ld_3", ld_rhs)
        S.wait("tensor", "dve", rT_cp.get(jd - 1, 0))
        for r in range(2):
            if r:
                S.wait("tensor", "dve", S.c["dve"])
            for jf4 in range(4):
                jf = 4 * r + jf4
                t_tr(PX[:, jf4 * 128:(jf4 + 1) * 128],
                     rhs32[:, jf * KB + jd * 128: jf * KB + jd * 128 + 128],
                     iden_f[:], inc=(jf4 == 3))
            S.wait("vector", "mm", S.c["mm"])
            v_op("tensor_copy", rhsT_all[:, jd * KB + r * 512: jd * KB + (r + 1) * 512],
                 PX[:, 0:512], inc=True)
        rj = rhsT_all[:, jd * KB:(jd + 1) * KB]
        v_op("tensor_scalar", sqb3[:], rj, 1.0, None, op0=OP.mult,
             op1=OP.max, accum_out=vc(160 + jd))
        v_op("tensor_scalar", sqb3[:], rj, -1.0, None, op0=OP.mult,
             op1=OP.max, accum_out=vc(168 + jd), inc=True)
        rT_cp[jd] = S.c["dve"]
    vpad(4)
    v_op("tensor_tensor", out=vecs[:, 176:184], in0=vecs[:, 160:168],
         in1=vecs[:, 168:176], op=OP.max)
    vpad(4)
    v_op("tensor_scalar", vecs[:, 184:192], vecs[:, 176:184], 1e-12,
         1.0 / 127.0, op0=OP.max, op1=OP.mult)
    vpad(4)
    v_op("reciprocal", vecs[:, 192:200], vecs[:, 184:192])
    vpad(8)
    for jd in range(8):
        rj = rhsT_all[:, jd * KB:(jd + 1) * KB]
        v_op("tensor_scalar", ffn_t[:], rj, vc(192 + jd), MAGIC,
             op0=OP.mult, op1=OP.add)
        v_op("tensor_scalar", qrhsT_jd[:], ffn_t[:], MAGIC, None,
             op0=OP.subtract, inc=True)
        S.wait("tensor", "dve", S.c["dve"])
        for jf in range(8):
            t_tr(AP(pxv.tensor, pxv.offset + jf * 128, [[1024, 128], [1, 128]]),
                 qrhsT_jd[:, jf * 128:(jf + 1) * 128], iden_bf[:], inc=(jf == 7))
        S.wait("vector", "mm", S.c["mm"])
        dstq = AP(qrhs_bf[:].tensor, qrhs_bf[:].offset + jd * 128,
                  [[8192, 128], [KB, 8], [1, 128]])
        srcq = AP(pxv.tensor, pxv.offset, [[1024, 128], [128, 8], [1, 128]])
        v_op("tensor_copy", dstq, srcq, inc=True)
    qrhs_done = S.c["dve"]
    if DEBUG:
        S.wait("gpsimd", "dve", qrhs_done)
        dma("gpsimd", "st", dbg_qr, qrhs_bf[:])
    S.wait("tensor", "dve", qrhs_done)
    t_tr(PX[0:8, 0:128], vecs[:, 184:192], iden_f[:], inc=True)
    S.wait("vector", "mm", S.c["mm"])
    v_op("tensor_copy", srow[:], PX[0:8, 0:128], inc=True)
    S.wait("sync", "dve", S.c["dve"])
    ld_srow = dma("sync", "ld_3", srow2[:], srow[:])
    S.wait("tensor", "ld_3", ld_srow)
    for s2 in range(2):
        if s2:
            S.wait("tensor", "dve", S.c["dve"])
        t_mm(BCP[:, 0:512], onesrow_f[0:1, 0:128],
             srow2[0:1, s2 * 512:(s2 + 1) * 512], True, True, inc=True)
        S.wait("vector", "mm", S.c["mm"])
        v_op("tensor_copy", RS_sb[:, s2 * 512:(s2 + 1) * 512], BCP[:, 0:512],
             inc=True)
    rs_done = S.c["dve"]

    yevac = {}
    ye_hist = {}
    for qt in range(16):
        if qt == 0:
            S.wait("tensor", "dve", p2_dve)
        pbq = PSA
        S.wait("tensor", "act", ye_hist.get(qt - 1, 0))
        for dh in range(2):
            for jpair in range(NPAIR):
                t_mm(pbq[:, dh * 512:dh * 512 + 512],
                     OT[:, jpair * NQ + qt * 128: jpair * NQ + qt * 128 + 128],
                     Wo_bf[:, jpair * KB + dh * 512: jpair * KB + dh * 512 + 512],
                     jpair == 0, jpair == NPAIR - 1, inc=(jpair == NPAIR - 1))
        wo_done = S.c["mm"]
        S.wait("scalar", "mm", wo_done)
        ye = a_op("activation", x2[:, qt * KB:(qt + 1) * KB], pbq[:], AF.Identity,
                  bias=0.0, scale=1.0, inc=True)
        ye_hist[qt] = ye
        ln_accums(x2[:, qt * KB:(qt + 1) * KB], qt, 16 + qt, sqb3)
        yevac[qt] = S.c["dve"]
    vpad(4)
    v_op("tensor_scalar", vecs[:, 320:336], vecs[:, 0:16], -1.0, None,
         op0=OP.mult, inc=True)
    ln2_rstd = ln_tail(0, 16, 32, 48, 64, 48, 16)

    x2_done = {}
    ym_done = {}
    ln2a = {}
    for qt in range(16):
        S.wait("sync", "dve", x2_done.get(qt - 1, 0))
        x3_ld = dma("sync", "ld_3", x3_sb[:], x_d[qt * 128:(qt + 1) * 128, :])
        S.wait("scalar", "dve", ym_done.get(qt - 2, 0))
        lbuf = ln3t if qt % 2 == 0 else ffn_t
        a_op("activation", ln2t[:], x2[:, qt * KB:(qt + 1) * KB],
             AF.Identity, bias=vc(320 + qt), scale=1.0)
        ln2a[qt] = a_op("activation", lbuf[:], ln2t[:], AF.Identity,
                        bias=0.0, scale=vc(48 + qt), inc=True)
        S.wait("vector", "ld_3", x3_ld)
        S.wait("vector", "act", ln2a[qt])
        v_op("tensor_tensor", out=x2[:, qt * KB:(qt + 1) * KB], in0=lbuf[:],
             in1=x3_sb[:], op=OP.add, inc=True)
        ym_done[qt] = S.c["dve"]
        S.wait("scalar", "dve", ym_done[qt])
        ln_accums(x2[:, qt * KB:(qt + 1) * KB], 80 + qt, 96 + qt, sqb3)
        x2_done[qt] = S.c["dve"]
    vpad(4)
    v_op("tensor_scalar", vecs[:, 336:352], vecs[:, 80:96], -1.0, None,
         op0=OP.mult, inc=True)
    ln3_rstd = ln_tail(80, 96, 112, 128, 144, 128, 16)
    S.wait("scalar", "dve", S.c["dve"])

    xm3_done = {}
    qsc_done = {}
    ffn_done_d = {}
    ffb_done = {}
    ln4t_done = {}
    out_done = {}
    out_done_dve = {}
    qlT_cp = {}
    ffa = {}

    for step in range(20):
        qt = step
        if qt < 16:
            if qt == 0:
                S.wait("scalar", "dve", qrhs_done)
            a_op("activation", ln2t[:], x2[:, qt * KB:(qt + 1) * KB],
                 AF.Identity, bias=vc(336 + qt), scale=1.0)
            xma = a_op("activation", xm3buf[qt % 3][:], ln2t[:],
                       AF.Identity, bias=0.0, scale=vc(128 + qt), inc=True)
            S.wait("vector", "act", xma)
            v_op("tensor_scalar", sqb3[:], xm3buf[qt % 3][:], 1.0, None,
                 op0=OP.mult, op1=OP.max, accum_out=vc(160 + qt))
            v_op("tensor_scalar", sqb3[:], xm3buf[qt % 3][:], -1.0, None,
                 op0=OP.mult, op1=OP.max, accum_out=vc(176 + qt), inc=True)
            xm3_done[qt] = S.c["dve"]
        j = step - 1
        if 0 <= j < 16:
            vpad(4)
            v_op("tensor_tensor", out=vc(192 + j), in0=vc(160 + j),
                 in1=vc(176 + j), op=OP.max)
            vpad(4)
            v_op("tensor_scalar", vc(224 + j), vc(192 + j), 1e-12, 1.0 / 127.0,
                 op0=OP.max, op1=OP.mult)
            vpad(4)
            v_op("reciprocal", vc(240 + j), vc(224 + j), inc=True)
            vpad(4)
            qsc_done[j] = S.c["dve"]
        k = step - 2
        if 0 <= k < 16:
            v_op("tensor_scalar", ffn_t[:], xm3buf[k % 3][:], vc(240 + k), MAGIC,
                 op0=OP.mult, op1=OP.add)
            v_op("tensor_scalar", qlhsq[:], ffn_t[:], MAGIC, None,
                 op0=OP.subtract, inc=True)
            S.wait("tensor", "dve", S.c["dve"])
            S.wait("tensor", "dve", ffb_done.get(k - 1, 0))
            pxv2 = bf16view(PX)
            for jf in range(8):
                t_tr(AP(pxv2.tensor, pxv2.offset + jf * 128, [[1024, 128], [1, 128]]),
                     qlhsq[:, jf * 128:(jf + 1) * 128], iden_bf[:], inc=(jf == 7))
            S.wait("vector", "mm", S.c["mm"])
            dstl = AP(qlhsT[:].tensor, qlhsT[:].offset + k * 128,
                      [[16384, 128], [NQ, 8], [1, 128]])
            srcl = AP(pxv2.tensor, pxv2.offset, [[1024, 128], [128, 8], [1, 128]])
            qlT_cp[k] = v_op("tensor_copy", dstl, srcl, inc=True)
            S.wait("tensor", "dve", qlT_cp[k])
            for dh in range(2):
                for jf in range(8):
                    t_mm(PSB[:, dh * 512:dh * 512 + 512],
                         qlhsT[:, jf * NQ + k * 128: jf * NQ + k * 128 + 128],
                         qrhs_bf[:, jf * KB + dh * 512: jf * KB + dh * 512 + 512],
                         jf == 0, jf == 7, inc=(jf == 7))
            ffn_done_d[k] = S.c["mm"]
            S.wait("vector", "mm", ffn_done_d[k])
            v_op("tensor_scalar", ffn_t[:], PSB[:], vc(224 + k), None, op0=OP.mult)
            v_op("tensor_tensor", out=ffnbuf[k % 3][:], in0=ffn_t[:], in1=RS_sb[:],
                 op=OP.mult, inc=True)
            ffb_done[k] = S.c["dve"]
            S.wait("scalar", "dve", ffb_done[k])
            ln_accums(ffnbuf[k % 3][:], 272 + (k % 4), 276 + (k % 4), sqb3)
        m = step - 3
        if 0 <= m < 16:
            q4 = m % 4
            cbase = 288 + q4 * 8
            vpad(4)
            v_op("tensor_tensor", out=vc(cbase), in0=vc(272 + q4),
                 in1=vc(272 + q4), op=OP.mult)
            vpad(4)
            v_op("tensor_tensor", out=vc(cbase + 1), in0=vc(276 + q4),
                 in1=vc(cbase), op=OP.subtract)
            vpad(4)
            v_op("tensor_scalar", vc(cbase + 2), vc(cbase + 1), EPS, None,
                 op0=OP.add)
            v_op("tensor_scalar", vc(352 + q4), vc(272 + q4), -1.0, None,
                 op0=OP.mult)
            vpad(2)
            mm1 = v_op("nop", inc=True)
            S.wait("scalar", "dve", mm1)
            apad(8)
            a_op("activation", vc(cbase + 3), vc(cbase + 2), AF.Ln)
            apad(1)
            a_op("activation", vc(cbase + 4), vc(cbase + 3), AF.Exp,
                 bias=0.0, scale=-0.5)
            apad(2)
            ln4t_done[m] = a_op("nop", inc=True)
        n = step - 4
        if 0 <= n < 16:
            q4 = n % 4
            cbase = 288 + q4 * 8
            S.wait("scalar", "dve", out_done_dve.get(n - 1, 0))
            apad(8)
            a_op("activation", ln2t[:], ffnbuf[n % 3][:], AF.Identity,
                 bias=vc(352 + q4), scale=1.0)
            ffa[n] = a_op("activation", ln3t[:], ln2t[:], AF.Identity,
                          bias=0.0, scale=vc(cbase + 4), inc=True)
            S.wait("vector", "act", ffa[n])
            S.wait("vector", "st", out_done.get(n - 2, 0))
            v_op("tensor_tensor", out=outst[n % 2][:], in0=ln3t[:],
                 in1=x2[:, n * KB:(n + 1) * KB], op=OP.add, inc=True)
            out_done_dve[n] = S.c["dve"]
            S.wait("gpsimd", "dve", S.c["dve"])
            out_done[n] = dma("gpsimd", "st", out_d[n * 128:(n + 1) * 128, :],
                              outst[n % 2][:])

    if DEBUG:
        S.wait("gpsimd", "dve", S.c["dve"])
        S.wait("gpsimd", "act", S.c["act"])
        S.wait("gpsimd", "mm", S.c["mm"])
        dma("gpsimd", "st", dbg_ot, OT[:])
        dma("gpsimd", "st", dbg_x2, x2[:])

    S.wait("gpsimd", "st", S.c["st"])

    with nc.Block() as block:
        @block.sync
        def _(e):
            for f in S.ops["sync"]:
                f(e)

        @block.gpsimd
        def _(e):
            for f in S.ops["gpsimd"]:
                f(e)

        @block.tensor
        def _(e):
            for f in S.ops["tensor"]:
                f(e)

        @block.vector
        def _(e):
            for f in S.ops["vector"]:
                f(e)

        @block.scalar
        def _(e):
            for f in S.ops["scalar"]:
                f(e)

    for g in reversed(sem_stack):
        g.__exit__(None, None, None)
    for g in reversed(psum_stack):
        g.__exit__(None, None, None)
    arena.__exit__(None, None, None)
    return nc




def kernel(L0_attn_hidden_states, L0_attn_Wq, L0_attn_Wk, L0_attn_Wv,
           L0_attn_Wo, L0_attn_rel_bias_table, L0_ffn_rhs):
    x = np.ascontiguousarray(np.asarray(L0_attn_hidden_states, dtype=np.float32))
    wq = np.ascontiguousarray(np.asarray(L0_attn_Wq, dtype=np.float32))
    wk = np.ascontiguousarray(np.asarray(L0_attn_Wk, dtype=np.float32))
    wv = np.ascontiguousarray(np.asarray(L0_attn_Wv, dtype=np.float32))
    wo = np.ascontiguousarray(np.asarray(L0_attn_Wo, dtype=np.float32))
    tbl = np.ascontiguousarray(np.asarray(L0_attn_rel_bias_table, dtype=np.float32))
    rhs = np.ascontiguousarray(np.asarray(L0_ffn_rhs, dtype=np.float32))

    iden = np.eye(128, dtype=np.float32)
    onescol = np.zeros((128, 64), np.float32)
    onescol[:, 0] = 1.0
    onesrow = np.ones((128, 128), np.float32)

    geo = {p: _host_geometry(p) for p in (0, 1)}

    nc = build_nc()
    in_maps = []
    for core in range(8):
        b, par = core // 2, core % 2
        xr = np.roll(x[b], -2048 * par, axis=0) if par else x[b]
        oh, sl, sh = geo[par]
        in_maps.append({
            "x": np.ascontiguousarray(xr),
            "wq": wq, "wk": wk, "wv": wv, "wo": wo, "rhs": rhs,
            "tbl": tbl, "ohall": oh, "sel_lo": sl, "sel_hi": sh,
            "iden": iden, "onescol": onescol, "onesrow": onesrow,
        })

    res = run_bass_kernel_spmd(nc, in_maps, core_ids=list(range(8)))
    if DEBUG:
        global _dbg
        _dbg = res.results

    out = np.empty((4, SQ, DML), np.float32)
    for core in range(8):
        b, par = core // 2, core % 2
        out[b, par * NQ:(par + 1) * NQ, :] = res.results[core]["out"]
    return out


# version 18
# speedup vs baseline: 1.0123x; 1.0123x over previous
import math
import numpy as np
import ml_dtypes

import concourse.bass as bass
import concourse.mybir as mybir
from concourse.bass_utils import run_bass_kernel_spmd
from bass_rust import AP

F32 = mybir.dt.float32
BF16 = mybir.dt.bfloat16
AF = mybir.ActivationFunctionType
OP = mybir.AluOpType

DEBUG = False
MAGIC = 12582912.0
CSTAB = 32.0
EPS = 1e-5

NH = 16
DK = 64
DML = 1024
SQ = 4096
NQ = 2048
NPAIR = 8
NCH = 32
NT = 4
KB = 1024



def _bucket(rel):
    rel = np.asarray(rel, dtype=np.int64)
    nb = 16
    rb = (rel > 0).astype(np.int64) * nb
    ar = np.abs(rel)
    max_exact = 8
    is_small = ar < max_exact
    arf = np.maximum(ar, 1).astype(np.float32)
    rl = max_exact + (
        np.log(arf / np.float32(max_exact))
        / np.float32(math.log(128.0 / max_exact))
        * (nb - max_exact)
    ).astype(np.int64)
    rl = np.minimum(rl, nb - 1)
    return rb + np.where(is_small, ar, rl)


def _win_list(t):
    out = []
    for w in range(6):
        c = 4 * t + w - 1
        if 0 <= c <= 31:
            if t == 3 and w == 5:
                out.append((c, 1, 511))
            else:
                out.append((c, 0, 128 * w + 511))
    if t == 0:
        out.append((31, 2, 511))
    return out


_WINSET = set()
for _t in range(NT):
    for _c, _s, _o in _win_list(_t):
        _WINSET.add((_t, _c))

OFF_B0, LEN_B0 = 0, 1280
OFF_X, LEN_X = 1280, 640
OFF_Y, LEN_Y = 1920, 640
NVCOL = 2560


def _host_geometry(parity):
    oh = np.zeros((32, NVCOL), np.float32)
    m = np.arange(LEN_B0)
    oh[_bucket(m - 639), OFF_B0 + m] = 1.0
    m = np.arange(LEN_X)
    if parity == 0:
        oh[_bucket(m + 1), OFF_X + m] = 1.0
    else:
        oh[15, OFF_X + m] = 1.0
    m = np.arange(LEN_Y)
    if parity == 0:
        oh[_bucket(m + 3457), OFF_Y + m] = 1.0
    else:
        oh[_bucket(m - 639), OFF_Y + m] = 1.0

    sel_lo = np.zeros((NT, NCH), np.float32)
    sel_hi = np.zeros((NT, NCH), np.float32)
    for t in range(NT):
        for c in range(NCH):
            if (t, c) in _WINSET:
                continue
            dmin = 128 * c - 512 * t - 511
            dmax = 128 * c - 512 * t + 127
            ds = np.arange(dmin, dmax + 1)
            if parity == 1 and c >= 16:
                ds = ds - 4096
            bs = np.unique(_bucket(ds))
            assert bs.size == 1 and bs[0] in (15, 31), (parity, t, c, bs)
            if bs[0] == 15:
                sel_lo[t, c] = 1.0
            else:
                sel_hi[t, c] = 1.0
    sl = np.broadcast_to(sel_lo.reshape(1, -1), (128, 128)).copy()
    sh = np.broadcast_to(sel_hi.reshape(1, -1), (128, 128)).copy()
    return oh, sl, sh




class Sched:
    ENGS = ("sync", "gpsimd", "tensor", "vector", "scalar")

    def __init__(self, sems):
        self.ops = {e: [] for e in self.ENGS}
        self.sems = sems
        self.c = {k: 0 for k in sems}

    def emit(self, eng, fn):
        self.ops[eng].append(fn)

    def wait(self, eng, sem, val):
        if val <= 0:
            return
        s = self.sems[sem]
        self.ops[eng].append(lambda e, s=s, v=val: e.wait_ge(s, v))

    def inc(self, sem, n=1):
        self.c[sem] += n
        return self.c[sem]



def build_nc():
    nc = bass.Bass("TRN2", target_bir_lowering=False)

    def din(name, shape, dt=F32):
        return nc.dram_tensor(name, shape, dt, kind="ExternalInput").ap()

    x_d = din("x", [SQ, DML])
    wq_d = din("wq", [DML, DML])
    wk_d = din("wk", [DML, DML])
    wv_d = din("wv", [DML, DML])
    wo_d = din("wo", [DML, DML])
    rhs_d = din("rhs", [DML, DML])
    tbl_d = din("tbl", [32, NH])
    oh_d = din("ohall", [32, NVCOL])
    sel_lo_d = din("sel_lo", [128, 128])
    sel_hi_d = din("sel_hi", [128, 128])
    iden_d = din("iden", [128, 128])
    onescol_d = din("onescol", [128, 64])
    onesrow_d = din("onesrow", [128, 128])

    out_d = nc.dram_tensor("out", [NQ, DML], F32, kind="ExternalOutput").ap()
    ik = "ExternalOutput" if DEBUG else "Internal"
    vecd_d = nc.dram_tensor("vecd", [NH, NVCOL], F32, kind=ik).ap()
    ktd_d = nc.dram_tensor("ktd", [NPAIR, 128, SQ], BF16, kind=ik).ap()
    qtd_d = nc.dram_tensor("qtd", [NPAIR, 128, NQ], BF16, kind=ik).ap()
    if DEBUG:
        dbg_ot = nc.dram_tensor("d_ot", [128, 16 * KB], BF16, kind="ExternalOutput").ap()
        dbg_x2 = nc.dram_tensor("d_x2", [128, 16 * KB], F32, kind="ExternalOutput").ap()
        dbg_bias = nc.dram_tensor("d_bias", [128, 2 * KB], F32, kind="ExternalOutput").ap()
        dbg_v = nc.dram_tensor("d_v", [128, 32 * KB], BF16, kind="ExternalOutput").ap()
        dbg_qr = nc.dram_tensor("d_qr", [128, 8 * KB], BF16, kind="ExternalOutput").ap()
        dbg_lt0 = nc.dram_tensor("d_lt0", [128, 4096], BF16, kind="ExternalOutput").ap()
        dbg_kq = nc.dram_tensor("d_kq", [128, NQ], BF16, kind="ExternalOutput").ap()

    sbase = int(nc.sbuf_base)
    arena_bytes = 229344 - sbase - 512
    arena = nc.sbuf_tensor("arena", [128, arena_bytes], mybir.dt.uint8)
    arena_h = arena.__enter__()
    BASE = (sbase + 63) // 64 * 64

    def sb(name, shape, dt, off):
        return nc.alloc_sbuf_tensor_at(name, shape, dt, offset=BASE + off)

    V_sb = sb("V_sb", [128, 32 * KB], BF16, 0x00000)
    OT = sb("OT", [128, 16 * KB], BF16, 0x10000)
    Wo_bf = sb("Wo_bf", [128, 8 * KB], BF16, 0x18000)
    biastc = sb("biastc", [128, 2 * KB], F32, 0x1C000)
    iden_bf = sb("iden_bf", [128, 128], BF16, 0x1E000)
    iden_f = sb("iden_f", [128, 128], F32, 0x1E100)
    onescol_bf = sb("onescol_bf", [128, 64], BF16, 0x1E300)
    onesrow_bf = sb("onesrow_bf", [128, 128], BF16, 0x1E380)
    onesrow_f = sb("onesrow_f", [128, 128], F32, 0x1E480)
    sel_lo_sb = sb("sel_lo_sb", [128, 128], F32, 0x1E680)
    sel_hi_sb = sb("sel_hi_sb", [128, 128], F32, 0x1E880)
    tbl_sb = sb("tbl_sb", [32, NH], F32, 0x1EA80)
    vecs = sb("vecs", [128, 384], F32, 0x1EB00)
    crow = sb("crow", [1, 32], F32, 0x1FC00)
    tmpA = sb("tmpA", [128, 128], F32, 0x1FE00)
    tmpB = sb("tmpB", [128, 128], F32, 0x20000)
    srow = sb("srow", [8, 128], F32, 0x20200)
    den_bf = sb("den_bf", [128, 512], BF16, 0x20400)
    rcp_f = sb("rcp_f", [128, 512], F32, 0x20800)
    BC_sb = sb("BC_sb", [128, 512], F32, 0x21000)
    x3_sb = sb("x3_sb", [128, KB], F32, 0x21400)
    kstage = [sb(f"kstage{i}", [128, 512], BF16, 0x22800 + i * 0x400) for i in range(2)]
    qstage = [sb(f"qstage{i}", [128, 512], BF16, 0x23000 + i * 0x400) for i in range(2)]
    ln2t = sb("ln2t", [128, KB], F32, 0x22400)
    PT_h = sb("PT_h", [128, KB], BF16, 0x23800)
    PT_g = sb("PT_g", [128, KB], BF16, 0x24000)
    Bz0 = [[sb(f"Bz0_{h}_{i}", [128, 1152], F32, 0x24800 + (h * 2 + i) * 0x1200)
            for i in range(2)] for h in range(2)]
    BzX = [[sb(f"BzX_{h}_{i}", [128, 512], F32, 0x29000 + (h * 2 + i) * 0x800)
            for i in range(2)] for h in range(2)]
    BzY = [[sb(f"BzY_{h}_{i}", [128, 512], F32, 0x2B000 + (h * 2 + i) * 0x800)
            for i in range(2)] for h in range(2)]
    KT_buf = [sb(f"KT_buf{i}", [128, SQ], BF16, 0x2D000 + i * 0x2000) for i in range(2)]
    QT_buf = [sb(f"QT_buf{i}", [128, NQ], BF16, 0x31000 + i * 0x1000) for i in range(2)]
    Wq_bf = sb("Wq_bf", [128, 8 * KB], BF16, 0x10000)
    Wk_bf = sb("Wk_bf", [128, 8 * KB], BF16, 0x14000)
    Wv_bf = sb("Wv_bf", [128, 8 * KB], BF16, 0x24800)
    ln1T = [sb(f"ln1T{i}", [128, 8 * 512], BF16, 0x29000 + i * 0x2000) for i in range(2)]
    xT_sb = [sb(f"xT{i}", [128, KB], F32, 0x2D000 + i * 0x1000) for i in range(2)]
    xm_sb = [sb(f"xm{i}", [128, KB], F32, 0x2F000 + i * 0x1000) for i in range(2)]
    ln1_bf = [sb(f"ln1bf{i}", [128, KB], BF16, 0x31000 + i * 0x800) for i in range(2)]
    sqb1 = sb("sqb1", [128, KB], F32, 0x32000)
    W32 = sb("W32", [128, 8 * KB], F32, 0x00000)
    ohall_sb = sb("ohall_sb", [32, NVCOL], F32, 0x08000)
    vecall_sb = sb("vecall_sb", [NH, NVCOL], F32, 0x0B000)
    x2 = sb("x2", [128, 16 * KB], F32, 0x00000)
    rhsT_all = sb("rhsT_all", [128, 8 * KB], F32, 0x00000)
    xm3buf = [sb(f"xm3_{i}", [128, KB], F32, 0x10000 + i * 0x1000) for i in range(3)]
    ffnbuf = [sb(f"ffnb_{i}", [128, KB], F32, 0x18000 + i * 0x1000) for i in range(3)]
    RS_sb = sb("RS_sb", [128, KB], F32, 0x1C000)
    sqb3 = sb("sqb3", [128, KB], F32, 0x1D000)
    ffn_t = sb("ffn_t", [128, KB], F32, 0x20400)
    qi32 = sb("qi32", [128, KB], mybir.dt.int32, 0x20400)
    srow2 = sb("srow2", [1, KB], F32, 0x25400)
    qlhsT = sb("qlhsT", [128, 16 * KB], BF16, 0x27400)
    rhs32 = sb("rhs32", [128, 8 * KB], F32, 0x27400)
    outst = [sb(f"outst{i}", [128, KB], F32, 0x25400 + i * 0x1000) for i in range(2)]
    ln3t = sb("ln3t", [128, KB], F32, 0x23400)
    qlhsq = sb("qlhsq", [128, KB], BF16, 0x24400)
    qrhsT_jd = sb("qrhsT_jd", [128, KB], BF16, 0x24C00)
    qrhs_bf = sb("qrhs_bf", [128, 8 * KB], BF16, 0x2F400)

    psum_stack = []
    def ps(name, shape):
        g = nc.psum_tensor(name, shape, F32)
        psum_stack.append(g)
        return g.__enter__()

    PSA = ps("PSA", [128, 1024])
    PSB = ps("PSB", [128, 1024])
    ACC = ps("ACC", [128, 512])
    DEN = ps("DEN", [128, 512])
    BCP = ps("BCP", [128, 512])
    PX = ps("PX", [128, 512])

    def bf16view(psap):
        return psap[:].bitcast(BF16)

    sem_stack = []
    sems = {}
    for name in ("ld_c", "ld_x", "ld_k", "ld_3", "st", "mm", "act", "dve"):
        g = nc.semaphore(name)
        sem_stack.append(g)
        sems[name] = g.__enter__()

    S = Sched(sems)

    def t_mm(out, lhsT, rhs, start, stop, inc=False):
        v = S.inc("mm") if inc else None
        def f(e, out=out, l=lhsT, r=rhs, st=start, sp=stop, v=v):
            i = e.matmul(out, l, r, start=st, stop=sp)
            if v is not None:
                i.then_inc(sems["mm"], 1)
        S.emit("tensor", f)
        return S.c["mm"]

    def t_tr(out, in_, iden, inc=False):
        v = S.inc("mm") if inc else None
        def f(e, out=out, i_=in_, idn=iden, v=v):
            i = e.transpose(out, i_, idn)
            if v is not None:
                i.then_inc(sems["mm"], 1)
        S.emit("tensor", f)
        return S.c["mm"]

    def v_op(meth, *args, inc=False, **kw):
        v = S.inc("dve") if inc else None
        def f(e, m=meth, a=args, kw=kw, v=v):
            i = getattr(e, m)(*a, **kw)
            if v is not None:
                i.then_inc(sems["dve"], 1)
        S.emit("vector", f)
        return S.c["dve"]

    def a_op(meth, *args, inc=False, **kw):
        v = S.inc("act") if inc else None
        def f(e, m=meth, a=args, kw=kw, v=v):
            i = getattr(e, m)(*a, **kw)
            if v is not None:
                i.then_inc(sems["act"], 1)
        S.emit("scalar", f)
        return S.c["act"]

    def vpad(n=4):
        for _ in range(n):
            v_op("tensor_scalar", vecs[:, 380:381], vecs[:, 381:382], 1.0, None,
                 op0=OP.mult)

    def apad(n=4):
        for _ in range(n):
            a_op("activation", vecs[:, 378:379], vecs[:, 379:380], AF.Identity,
                 bias=0.0, scale=1.0)

    def dma(queue, sem, out, in_, slow=False):
        S.inc(sem, 16)
        def f(e, o=out, i=in_, s=sems[sem], slow=slow):
            kw = {"allow_slow_non_contiguous": True} if slow else {}
            e.dma_start(out=o, in_=i, **kw).then_inc(s, 16)
        S.emit(queue, f)
        return S.c[sem]

    vc = lambda c: vecs[:, c:c + 1]

    def ln_tail(mc, qc, m2c, varc, vepsc, rc, n):
        vpad(4)
        v_op("tensor_tensor", out=vecs[:, m2c:m2c + n], in0=vecs[:, mc:mc + n],
             in1=vecs[:, mc:mc + n], op=OP.mult)
        vpad(4)
        v_op("tensor_tensor", out=vecs[:, varc:varc + n], in0=vecs[:, qc:qc + n],
             in1=vecs[:, m2c:m2c + n], op=OP.subtract)
        vpad(4)
        v_op("tensor_scalar", vecs[:, vepsc:vepsc + n], vecs[:, varc:varc + n],
             EPS, None, op0=OP.add)
        vpad(4, )
        m1 = v_op("nop", inc=True)
        S.wait("scalar", "dve", m1)
        apad(8)
        a_op("activation", vecs[:, m2c:m2c + n], vecs[:, vepsc:vepsc + n], AF.Ln)
        apad(1)
        a_op("activation", vecs[:, rc:rc + n], vecs[:, m2c:m2c + n], AF.Exp,
             bias=0.0, scale=-0.5)
        apad(2)
        return a_op("nop", inc=True)

    def ln_accums(src_ap, mcol, qcol, sq):
        S.wait("scalar", "dve", S.c["dve"])
        sq_a = a_op("activation", sq[:], src_ap, AF.Square, inc=True)
        S.wait("vector", "act", sq_a)
        v_op("tensor_scalar", sq[:], sq[:], 1.0 / 1024.0, None, op0=OP.mult,
             op1=OP.add, accum_out=vc(qcol))
        v_op("tensor_scalar", sq[:], src_ap, 1.0 / 1024.0, None, op0=OP.mult,
             op1=OP.add, accum_out=vc(mcol), inc=True)
        return S.c["dve"]

    dma("sync", "ld_c", iden_f[:], iden_d)
    dma("sync", "ld_c", tmpA[:, 0:64], onescol_d)
    dma("sync", "ld_c", onesrow_f[:], onesrow_d)
    dma("sync", "ld_c", sel_lo_sb[:], sel_lo_d)
    dma("sync", "ld_c", sel_hi_sb[:], sel_hi_d)
    dma("sync", "ld_c", tbl_sb[:], tbl_d)
    ld0 = dma("sync", "ld_c", ohall_sb[:], oh_d)

    S.wait("vector", "ld_c", ld0)
    v_op("tensor_copy", iden_bf[:], iden_f[:])
    v_op("tensor_copy", onescol_bf[:], tmpA[:, 0:64])
    v_op("tensor_copy", onesrow_bf[:], onesrow_f[:], inc=True)

    wcast = {}
    for nm, d_ap, dst in (("wq", wq_d, Wq_bf), ("wk", wk_d, Wk_bf),
                          ("wv", wv_d, Wv_bf), ("wo", wo_d, Wo_bf)):
        src = AP(d_ap.tensor, 0, [[1024, 128], [131072, 8], [1, 1024]])
        if wcast:
            S.wait("sync", "dve", S.c["dve"])
        lw = dma("sync", "ld_c", W32[:], src)
        S.wait("vector", "ld_c", lw)
        wcast[nm] = ("dve", v_op("tensor_copy", dst[:], W32[:], inc=True))

    S.wait("tensor", "ld_c", ld0)
    for s5 in range(5):
        if s5:
            S.wait("tensor", "dve", S.c["dve"])
        t_mm(PX[0:16, 0:512], tbl_sb[:, 0:NH], ohall_sb[:, s5 * 512:(s5 + 1) * 512],
             True, True, inc=True)
        S.wait("vector", "mm", S.c["mm"])
        v_op("tensor_copy", vecall_sb[:, s5 * 512:(s5 + 1) * 512], PX[0:16, 0:512],
             inc=True)
    vec_done = S.c["dve"]
    S.wait("gpsimd", "dve", vec_done)
    st_vec = dma("gpsimd", "st", vecd_d, vecall_sb[:])
    S.wait("sync", "dve", vec_done)
    va = vecall_sb[:]
    crow_src = AP(va.tensor, va.offset + OFF_B0, [[NVCOL, 16], [1278, 2]])
    ldc2 = dma("sync", "ld_c", crow[:], crow_src, slow=True)
    S.wait("tensor", "ld_c", ldc2)
    bcc_mm = t_mm(DEN[0:128, 0:32], onesrow_f[0:1, 0:128], crow[0:1, 0:32],
                  True, True, inc=True)
    S.wait("vector", "mm", bcc_mm)
    vpad(12)
    for h in range(NH):
        v_op("tensor_scalar", tmpA[:], sel_lo_sb[:], DEN[:, 2 * h:2 * h + 1],
             -CSTAB, op0=OP.mult, op1=OP.add)
        v_op("tensor_scalar", tmpB[:], sel_hi_sb[:], DEN[:, 2 * h + 1:2 * h + 2],
             None, op0=OP.mult)
        vpad(2)
        v_op("tensor_tensor", out=biastc[:, h * 128:(h + 1) * 128], in0=tmpA[:],
             in1=tmpB[:], op=OP.add, inc=True)
    biastc_done = S.c["dve"]
    if DEBUG:
        S.wait("gpsimd", "dve", biastc_done)
        dma("gpsimd", "st", dbg_bias, biastc[:])

    xa_ld = {}
    statsA = {}
    for i in range(32):
        S.wait("sync", "dve", statsA.get(i - 2, 0))
        xa_ld[i] = dma("sync", "ld_x", xT_sb[i % 2][:], x_d[i * 128:(i + 1) * 128, :])
        S.wait("scalar", "ld_x", xa_ld[i])
        statsA[i] = ln_accums(xT_sb[i % 2][:], i, 32 + i, sqb1)
    vpad(4)
    v_op("tensor_scalar", vecs[:, 160:192], vecs[:, 0:32], -1.0, None,
         op0=OP.mult, inc=True)
    ln1_rstd_done = ln_tail(0, 32, 64, 96, 128, 96, 32)

    transp_done = {}
    pxcopy = {}
    xb_ld = {}
    xm_done = {}
    ap_done = {}
    ln1T_done = {}
    kevac = {}
    kstore = {}
    qevac = {}
    qstore = {}
    vevac = {}

    S.wait("scalar", "dve", S.c["dve"])
    p1a_dve = statsA[31]
    for s in range(8):
        for sub in range(4):
            i = 4 * s + sub
            S.wait("sync", "act", ap_done.get(i - 2, 0))
            if i < 2:
                S.wait("sync", "dve", p1a_dve)
            xb_ld[i] = dma("sync", "ld_x", xT_sb[i % 2][:],
                           x_d[i * 128:(i + 1) * 128, :])
            S.wait("scalar", "ld_x", xb_ld[i])
            S.wait("scalar", "mm", transp_done.get(i - 2, 0))
            a_op("activation", xm_sb[i % 2][:], xT_sb[i % 2][:],
                 AF.Identity, bias=vc(160 + i), scale=1.0)
            ap_done[i] = a_op("activation", ln1_bf[i % 2][:], xm_sb[i % 2][:],
                              AF.Identity, bias=0.0, scale=vc(96 + i), inc=True)
            S.wait("tensor", "act", ap_done[i])
            S.wait("tensor", "dve", pxcopy.get(i - 1, 0))
            pxv = bf16view(PX)
            for jf in range(8):
                t_tr(AP(pxv.tensor, pxv.offset + jf * 128, [[1024, 128], [1, 128]]),
                     ln1_bf[i % 2][:, jf * 128:(jf + 1) * 128], iden_bf[:],
                     inc=(jf == 7))
            transp_done[i] = S.c["mm"]
            S.wait("vector", "mm", transp_done[i])
            lt = ln1T[s % 2][:]
            dst = AP(lt.tensor, lt.offset + sub * 128, [[4096, 128], [512, 8], [1, 128]])
            srcp = AP(pxv.tensor, pxv.offset, [[1024, 128], [128, 8], [1, 128]])
            v_op("tensor_copy", dst, srcp, inc=True)
            pxcopy[i] = S.c["dve"]
        ln1T_done[s] = S.c["dve"]
        if DEBUG and s == 0:
            S.wait("gpsimd", "dve", ln1T_done[0])
            dma("gpsimd", "st", dbg_lt0, ln1T[0][:])

        for jp in range(NPAIR):
            S.wait("tensor", "dve", max(ln1T_done[s],
                                        kevac.get((s, jp - 2),
                                                  kevac.get((s - 1, jp + 6), 0))))
            S.wait("tensor", "dve", wcast["wk"][1])
            for jf in range(8):
                t_mm(PSA[:, (jp % 2) * 512:(jp % 2) * 512 + 512],
                     Wk_bf[:, jf * KB + jp * 128: jf * KB + jp * 128 + 128],
                     ln1T[s % 2][:, jf * 512:(jf + 1) * 512],
                     jf == 0, jf == 7, inc=(jf == 7))
            ktmm = S.c["mm"]
            S.wait("vector", "mm", ktmm)
            S.wait("vector", "st",
                   kstore.get((s, jp - 2), kstore.get((s - 1, jp + 6), 0)))
            kevac[(s, jp)] = v_op("tensor_copy", kstage[jp % 2][:],
                                  PSA[:, (jp % 2) * 512:(jp % 2) * 512 + 512],
                                  inc=True)
            S.wait("gpsimd", "dve", kevac[(s, jp)])
            dstk = AP(ktd_d.tensor, jp * 128 * SQ + s * 512, [[SQ, 128], [1, 512]])
            kstore[(s, jp)] = dma("gpsimd", "st", dstk, kstage[jp % 2][:])
        for sub in range(4):
            for dh in range(2):
                idx = sub * 2 + dh
                if s == 4 and idx == 0:
                    S.wait("vector", "st", st_vec)
                    S.wait("vector", "ld_c", S.c["ld_c"])
                S.wait("tensor", "dve",
                       vevac.get((s, idx - 2), vevac.get((s - 1, idx + 6), 0)))
                S.wait("tensor", "dve", wcast["wv"][1])
                for jf in range(8):
                    t_mm(PSB[:, (idx % 2) * 512:(idx % 2) * 512 + 512],
                         ln1T[s % 2][:, jf * 512 + sub * 128: jf * 512 + sub * 128 + 128],
                         Wv_bf[:, jf * KB + dh * 512: jf * KB + dh * 512 + 512],
                         jf == 0, jf == 7, inc=(jf == 7))
                vmm = S.c["mm"]
                S.wait("vector", "mm", vmm)
                vevac[(s, idx)] = v_op(
                    "tensor_copy",
                    V_sb[:, (4 * s + sub) * KB + dh * 512:(4 * s + sub) * KB + dh * 512 + 512],
                    PSB[:, (idx % 2) * 512:(idx % 2) * 512 + 512], inc=True)
        if s < 4:
            for jp in range(NPAIR):
                pb = ACC if jp % 2 == 0 else DEN
                S.wait("tensor", "dve",
                       max(qevac.get((s, jp - 2), qevac.get((s - 1, jp + 6), 0)),
                           biastc_done))
                S.wait("tensor", "dve", wcast["wq"][1])
                for jf in range(8):
                    t_mm(pb[:, 0:512],
                         Wq_bf[:, jf * KB + jp * 128: jf * KB + jp * 128 + 128],
                         ln1T[s % 2][:, jf * 512:(jf + 1) * 512],
                         jf == 0, jf == 7, inc=(jf == 7))
                qmm = S.c["mm"]
                S.wait("vector", "mm", qmm)
                S.wait("vector", "st",
                       qstore.get((s, jp - 2), qstore.get((s - 1, jp + 6), 0)))
                qevac[(s, jp)] = v_op("tensor_copy", qstage[jp % 2][:], pb[:, 0:512],
                                      inc=True)
                S.wait("gpsimd", "dve", qevac[(s, jp)])
                dstq = AP(qtd_d.tensor, jp * 128 * NQ + s * 512, [[NQ, 128], [1, 512]])
                qstore[(s, jp)] = dma("gpsimd", "st", dstq, qstage[jp % 2][:])

    p1_stores = S.c["st"]
    p1_mm = S.c["mm"]
    p1_dve = S.c["dve"]
    if DEBUG:
        S.wait("gpsimd", "dve", p1_dve)
        dma("gpsimd", "st", dbg_v, V_sb[:])
        S.wait("gpsimd", "st", p1_stores + 16)
        dma("gpsimd", "st", dbg_kq, AP(qtd_d.tensor, 0, [[NQ, 128], [1, NQ]]))


    k_ld = {}
    pair_end_mm = {-2: 0, -1: 0}

    def issue_pair_loads(jp):
        b2 = jp % 2
        if jp <= 1:
            S.wait("sync", "st", p1_stores)
            S.wait("sync", "mm", p1_mm)
        else:
            S.wait("sync", "mm", pair_end_mm[jp - 2])
        dma("sync", "ld_k", KT_buf[b2][:],
            AP(ktd_d.tensor, jp * 128 * SQ, [[SQ, 128], [1, SQ]]))
        dma("sync", "ld_k", QT_buf[b2][:],
            AP(qtd_d.tensor, jp * 128 * NQ, [[NQ, 128], [1, NQ]]))
        for hh in range(2):
            h = 2 * jp + hh
            S.wait("sync", "st", st_vec)
            dma("sync", "ld_k", Bz0[hh][b2][:],
                AP(vecd_d.tensor, h * NVCOL + OFF_B0, [[1, 128], [1, 1152]]))
            dma("sync", "ld_k", BzX[hh][b2][:],
                AP(vecd_d.tensor, h * NVCOL + OFF_X, [[1, 128], [1, 512]]))
            k_ld[jp] = dma("sync", "ld_k", BzY[hh][b2][:],
                           AP(vecd_d.tensor, h * NVCOL + OFF_Y, [[1, 128], [1, 512]]))

    issue_pair_loads(0)
    issue_pair_loads(1)

    exp_done = {}
    attnv_done = {}
    otevac = {}
    pend_epi = [None]

    def emit_attnv(jp, b2, t, cp):
        cgp = t * NCH + cp
        parp = cgp % 2
        S.wait("tensor", "act", exp_done[(jp, cgp)])
        if cp == 0:
            S.wait("tensor", "dve",
                   otevac.get((jp, t - 1), otevac.get((jp - 1, NT - 1), p1_dve)))
        st_ = cp == 0
        sp_ = cp == NCH - 1
        t_mm(ACC[0:64, :], V_sb[:, cp * KB + jp * 128: cp * KB + jp * 128 + 64],
             PT_h[:, parp * 512:parp * 512 + 512], st_, sp_)
        t_mm(ACC[64:128, :], V_sb[:, cp * KB + jp * 128 + 64: cp * KB + jp * 128 + 128],
             PT_g[:, parp * 512:parp * 512 + 512], st_, sp_)
        t_mm(DEN[0:64, :], onescol_bf[:], PT_h[:, parp * 512:parp * 512 + 512],
             st_, sp_)
        t_mm(DEN[64:128, :], onescol_bf[:], PT_g[:, parp * 512:parp * 512 + 512],
             st_, sp_, inc=True)
        attnv_done[(jp, cgp)] = S.c["mm"]

    for jp in range(NPAIR):
        b2 = jp % 2
        for t in range(NT):
            wins = dict((c, (srcid, off)) for c, srcid, off in _win_list(t))
            add_done = {}
            def emit_scores(c):
                cg = t * NCH + c
                par = cg % 2
                S.wait("tensor", "act",
                       exp_done.get((jp, cg - 2),
                                    exp_done.get((jp - 1, 4 * NCH + cg - 2), 0)))
                if cg == 0:
                    S.wait("tensor", "ld_k", k_ld[jp])
                    S.wait("tensor", "dve", p1_dve)
                t_mm(PSA[:, par * 512:par * 512 + 512],
                     KT_buf[b2][0:64, c * 128:(c + 1) * 128],
                     QT_buf[b2][0:64, t * 512:(t + 1) * 512], True, True)
                t_mm(PSB[:, par * 512:par * 512 + 512],
                     KT_buf[b2][64:128, c * 128:(c + 1) * 128],
                     QT_buf[b2][64:128, t * 512:(t + 1) * 512], True, True,
                     inc=True)
                return S.c["mm"]

            def emit_expadd(c, sc_mm):
                cg = t * NCH + c
                par = cg % 2
                if c in wins:
                    srcid, off = wins[c]
                    S.wait("vector", "mm", sc_mm)
                    for hh, PS in ((0, PSA), (1, PSB)):
                        bt = (Bz0, BzX, BzY)[srcid][hh][b2]
                        stride = 1152 if srcid == 0 else 512
                        a0 = bt[:]
                        rev = AP(a0.tensor, a0.offset + off, [[stride, 128], [-1, 512]])
                        v_op("tensor_tensor",
                             out=PS[:, par * 512:par * 512 + 512],
                             in0=PS[:, par * 512:par * 512 + 512],
                             in1=rev, op=OP.add, inc=(hh == 1))
                    add_done[c] = S.c["dve"]
                S.wait("scalar", "mm", sc_mm)
                if c in wins:
                    S.wait("scalar", "dve", add_done[c])
                S.wait("scalar", "mm",
                       attnv_done.get((jp, cg - 2),
                                      attnv_done.get((jp - 1, 4 * NCH + cg - 2), 0)))
                for hh, PS, PT in ((0, PSA, PT_h), (1, PSB, PT_g)):
                    h = 2 * jp + hh
                    col = h * 128 + t * 32 + c
                    a_op("activation", PT[:, par * 512:par * 512 + 512],
                         PS[:, par * 512:par * 512 + 512], AF.Exp,
                         bias=biastc[:, col:col + 1], scale=1.0, inc=(hh == 1))
                exp_done[(jp, cg)] = S.c["act"]

            for c in range(NCH + 2):
                if c < NCH:
                    sc0 = emit_scores(c)
                if c == 2 and pend_epi[0] is not None:
                    pend_epi[0]()
                    pend_epi[0] = None
                if c >= 2:
                    emit_attnv(jp, b2, t, c - 2)
                if c < NCH:
                    emit_expadd(c, sc0)
            S.wait("vector", "mm", attnv_done[(jp, t * NCH + NCH - 1)])
            v_op("reciprocal", rcp_f[:], DEN[:])
            v_op("tensor_copy", den_bf[0:1, :], rcp_f[0:1, :])
            v_op("tensor_copy", den_bf[64:65, :], rcp_f[64:65, :], inc=True)
            den_done = S.c["dve"]

            def mk_epi(jp=jp, t=t, den_done=den_done):
                def epi():
                    S.wait("tensor", "dve", den_done)
                    t_mm(BCP[0:64, :], onesrow_bf[0:1, 0:64], den_bf[0:1, :], True, True)
                    t_mm(BCP[64:128, :], onesrow_bf[64:65, 0:64], den_bf[64:65, :],
                         True, True, inc=True)
                    bcast_done = S.c["mm"]
                    S.wait("vector", "mm", bcast_done)
                    v_op("tensor_copy", BC_sb[:], BCP[:])
                    v_op("tensor_tensor",
                         out=OT[:, jp * NQ + t * 512: jp * NQ + t * 512 + 512],
                         in0=ACC[:], in1=BC_sb[:], op=OP.mult, inc=True)
                    otevac[(jp, t)] = S.c["dve"]
                return epi
            pend_epi[0] = mk_epi()
        pair_end_mm[jp] = S.c["mm"]
        if jp + 2 < NPAIR:
            issue_pair_loads(jp + 2)
    pend_epi[0]()
    pend_epi[0] = None

    p2_dve = S.c["dve"]
    S.wait("sync", "mm", pair_end_mm[NPAIR - 1])
    S.wait("sync", "dve", p2_dve)
    rsrc = AP(rhs_d.tensor, 0, [[1024, 128], [131072, 8], [1, 1024]])
    ld_rhs = dma("sync", "ld_3", rhs32[:], rsrc)

    pxv = bf16view(PX)
    rT_cp = {}
    for jd in range(8):
        S.wait("tensor", "ld_3", ld_rhs)
        S.wait("tensor", "dve", rT_cp.get(jd - 1, 0))
        for r in range(2):
            if r:
                S.wait("tensor", "dve", S.c["dve"])
            for jf4 in range(4):
                jf = 4 * r + jf4
                t_tr(PX[:, jf4 * 128:(jf4 + 1) * 128],
                     rhs32[:, jf * KB + jd * 128: jf * KB + jd * 128 + 128],
                     iden_f[:], inc=(jf4 == 3))
            S.wait("vector", "mm", S.c["mm"])
            v_op("tensor_copy", rhsT_all[:, jd * KB + r * 512: jd * KB + (r + 1) * 512],
                 PX[:, 0:512], inc=True)
        rj = rhsT_all[:, jd * KB:(jd + 1) * KB]
        v_op("tensor_scalar", sqb3[:], rj, 1.0, None, op0=OP.mult,
             op1=OP.max, accum_out=vc(160 + jd))
        v_op("tensor_scalar", sqb3[:], rj, -1.0, None, op0=OP.mult,
             op1=OP.max, accum_out=vc(168 + jd), inc=True)
        rT_cp[jd] = S.c["dve"]
    vpad(4)
    v_op("tensor_tensor", out=vecs[:, 176:184], in0=vecs[:, 160:168],
         in1=vecs[:, 168:176], op=OP.max)
    vpad(4)
    v_op("tensor_scalar", vecs[:, 184:192], vecs[:, 176:184], 1e-12,
         1.0 / 127.0, op0=OP.max, op1=OP.mult)
    vpad(4)
    v_op("reciprocal", vecs[:, 192:200], vecs[:, 184:192])
    vpad(8)
    for jd in range(8):
        rj = rhsT_all[:, jd * KB:(jd + 1) * KB]
        v_op("tensor_scalar", qi32[:], rj, vc(192 + jd), None, op0=OP.mult)
        v_op("tensor_copy", qrhsT_jd[:], qi32[:], inc=True)
        S.wait("tensor", "dve", S.c["dve"])
        for jf in range(8):
            t_tr(AP(pxv.tensor, pxv.offset + jf * 128, [[1024, 128], [1, 128]]),
                 qrhsT_jd[:, jf * 128:(jf + 1) * 128], iden_bf[:], inc=(jf == 7))
        S.wait("vector", "mm", S.c["mm"])
        dstq = AP(qrhs_bf[:].tensor, qrhs_bf[:].offset + jd * 128,
                  [[8192, 128], [KB, 8], [1, 128]])
        srcq = AP(pxv.tensor, pxv.offset, [[1024, 128], [128, 8], [1, 128]])
        v_op("tensor_copy", dstq, srcq, inc=True)
    qrhs_done = S.c["dve"]
    if DEBUG:
        S.wait("gpsimd", "dve", qrhs_done)
        dma("gpsimd", "st", dbg_qr, qrhs_bf[:])
    S.wait("tensor", "dve", qrhs_done)
    t_tr(PX[0:8, 0:128], vecs[:, 184:192], iden_f[:], inc=True)
    S.wait("vector", "mm", S.c["mm"])
    v_op("tensor_copy", srow[:], PX[0:8, 0:128], inc=True)
    S.wait("sync", "dve", S.c["dve"])
    ld_srow = dma("sync", "ld_3", srow2[:], srow[:])
    S.wait("tensor", "ld_3", ld_srow)
    for s2 in range(2):
        if s2:
            S.wait("tensor", "dve", S.c["dve"])
        t_mm(BCP[:, 0:512], onesrow_f[0:1, 0:128],
             srow2[0:1, s2 * 512:(s2 + 1) * 512], True, True, inc=True)
        S.wait("vector", "mm", S.c["mm"])
        v_op("tensor_copy", RS_sb[:, s2 * 512:(s2 + 1) * 512], BCP[:, 0:512],
             inc=True)
    rs_done = S.c["dve"]

    yevac = {}
    ye_hist = {}
    for qt in range(16):
        if qt == 0:
            S.wait("tensor", "dve", p2_dve)
        pbq = PSA
        S.wait("tensor", "act", ye_hist.get(qt - 1, 0))
        for dh in range(2):
            for jpair in range(NPAIR):
                t_mm(pbq[:, dh * 512:dh * 512 + 512],
                     OT[:, jpair * NQ + qt * 128: jpair * NQ + qt * 128 + 128],
                     Wo_bf[:, jpair * KB + dh * 512: jpair * KB + dh * 512 + 512],
                     jpair == 0, jpair == NPAIR - 1, inc=(jpair == NPAIR - 1))
        wo_done = S.c["mm"]
        S.wait("scalar", "mm", wo_done)
        ye = a_op("activation", x2[:, qt * KB:(qt + 1) * KB], pbq[:], AF.Identity,
                  bias=0.0, scale=1.0, inc=True)
        ye_hist[qt] = ye
        ln_accums(x2[:, qt * KB:(qt + 1) * KB], qt, 16 + qt, sqb3)
        yevac[qt] = S.c["dve"]
    vpad(4)
    v_op("tensor_scalar", vecs[:, 320:336], vecs[:, 0:16], -1.0, None,
         op0=OP.mult, inc=True)
    ln2_rstd = ln_tail(0, 16, 32, 48, 64, 48, 16)

    x2_done = {}
    ym_done = {}
    ln2a = {}
    for qt in range(16):
        S.wait("sync", "dve", x2_done.get(qt - 1, 0))
        x3_ld = dma("sync", "ld_3", x3_sb[:], x_d[qt * 128:(qt + 1) * 128, :])
        S.wait("scalar", "dve", ym_done.get(qt - 2, 0))
        lbuf = ln3t if qt % 2 == 0 else ffn_t
        a_op("activation", ln2t[:], x2[:, qt * KB:(qt + 1) * KB],
             AF.Identity, bias=vc(320 + qt), scale=1.0)
        ln2a[qt] = a_op("activation", lbuf[:], ln2t[:], AF.Identity,
                        bias=0.0, scale=vc(48 + qt), inc=True)
        S.wait("vector", "ld_3", x3_ld)
        S.wait("vector", "act", ln2a[qt])
        v_op("tensor_tensor", out=x2[:, qt * KB:(qt + 1) * KB], in0=lbuf[:],
             in1=x3_sb[:], op=OP.add, inc=True)
        ym_done[qt] = S.c["dve"]
        S.wait("scalar", "dve", ym_done[qt])
        ln_accums(x2[:, qt * KB:(qt + 1) * KB], 80 + qt, 96 + qt, sqb3)
        x2_done[qt] = S.c["dve"]
    vpad(4)
    v_op("tensor_scalar", vecs[:, 336:352], vecs[:, 80:96], -1.0, None,
         op0=OP.mult, inc=True)
    ln3_rstd = ln_tail(80, 96, 112, 128, 144, 128, 16)
    S.wait("scalar", "dve", S.c["dve"])

    xm3_done = {}
    qsc_done = {}
    ffn_done_d = {}
    ffb_done = {}
    ln4t_done = {}
    out_done = {}
    out_done_dve = {}
    qlT_cp = {}
    ffa = {}

    for step in range(20):
        qt = step
        if qt < 16:
            if qt == 0:
                S.wait("scalar", "dve", qrhs_done)
            a_op("activation", ln2t[:], x2[:, qt * KB:(qt + 1) * KB],
                 AF.Identity, bias=vc(336 + qt), scale=1.0)
            xma = a_op("activation", xm3buf[qt % 3][:], ln2t[:],
                       AF.Identity, bias=0.0, scale=vc(128 + qt), inc=True)
            S.wait("vector", "act", xma)
            v_op("tensor_scalar", sqb3[:], xm3buf[qt % 3][:], 1.0, None,
                 op0=OP.mult, op1=OP.max, accum_out=vc(160 + qt))
            v_op("tensor_scalar", sqb3[:], xm3buf[qt % 3][:], -1.0, None,
                 op0=OP.mult, op1=OP.max, accum_out=vc(176 + qt), inc=True)
            xm3_done[qt] = S.c["dve"]
        j = step - 1
        if 0 <= j < 16:
            vpad(4)
            v_op("tensor_tensor", out=vc(192 + j), in0=vc(160 + j),
                 in1=vc(176 + j), op=OP.max)
            vpad(4)
            v_op("tensor_scalar", vc(224 + j), vc(192 + j), 1e-12, 1.0 / 127.0,
                 op0=OP.max, op1=OP.mult)
            vpad(4)
            v_op("reciprocal", vc(240 + j), vc(224 + j), inc=True)
            vpad(4)
            qsc_done[j] = S.c["dve"]
        k = step - 2
        if 0 <= k < 16:
            v_op("tensor_scalar", qi32[:], xm3buf[k % 3][:], vc(240 + k), None,
                 op0=OP.mult)
            v_op("tensor_copy", qlhsq[:], qi32[:], inc=True)
            S.wait("tensor", "dve", S.c["dve"])
            S.wait("tensor", "dve", ffb_done.get(k - 1, 0))
            pxv2 = bf16view(PX)
            for jf in range(8):
                t_tr(AP(pxv2.tensor, pxv2.offset + jf * 128, [[1024, 128], [1, 128]]),
                     qlhsq[:, jf * 128:(jf + 1) * 128], iden_bf[:], inc=(jf == 7))
            S.wait("vector", "mm", S.c["mm"])
            dstl = AP(qlhsT[:].tensor, qlhsT[:].offset + k * 128,
                      [[16384, 128], [NQ, 8], [1, 128]])
            srcl = AP(pxv2.tensor, pxv2.offset, [[1024, 128], [128, 8], [1, 128]])
            qlT_cp[k] = v_op("tensor_copy", dstl, srcl, inc=True)
            S.wait("tensor", "dve", qlT_cp[k])
            for dh in range(2):
                for jf in range(8):
                    t_mm(PSB[:, dh * 512:dh * 512 + 512],
                         qlhsT[:, jf * NQ + k * 128: jf * NQ + k * 128 + 128],
                         qrhs_bf[:, jf * KB + dh * 512: jf * KB + dh * 512 + 512],
                         jf == 0, jf == 7, inc=(jf == 7))
            ffn_done_d[k] = S.c["mm"]
            S.wait("vector", "mm", ffn_done_d[k])
            v_op("tensor_scalar", ffn_t[:], PSB[:], vc(224 + k), None, op0=OP.mult)
            v_op("tensor_tensor", out=ffnbuf[k % 3][:], in0=ffn_t[:], in1=RS_sb[:],
                 op=OP.mult, inc=True)
            ffb_done[k] = S.c["dve"]
            S.wait("scalar", "dve", ffb_done[k])
            ln_accums(ffnbuf[k % 3][:], 272 + (k % 4), 276 + (k % 4), sqb3)
        m = step - 3
        if 0 <= m < 16:
            q4 = m % 4
            cbase = 288 + q4 * 8
            vpad(4)
            v_op("tensor_tensor", out=vc(cbase), in0=vc(272 + q4),
                 in1=vc(272 + q4), op=OP.mult)
            vpad(4)
            v_op("tensor_tensor", out=vc(cbase + 1), in0=vc(276 + q4),
                 in1=vc(cbase), op=OP.subtract)
            vpad(4)
            v_op("tensor_scalar", vc(cbase + 2), vc(cbase + 1), EPS, None,
                 op0=OP.add)
            v_op("tensor_scalar", vc(352 + q4), vc(272 + q4), -1.0, None,
                 op0=OP.mult)
            vpad(2)
            mm1 = v_op("nop", inc=True)
            S.wait("scalar", "dve", mm1)
            apad(8)
            a_op("activation", vc(cbase + 3), vc(cbase + 2), AF.Ln)
            apad(1)
            a_op("activation", vc(cbase + 4), vc(cbase + 3), AF.Exp,
                 bias=0.0, scale=-0.5)
            apad(2)
            ln4t_done[m] = a_op("nop", inc=True)
        n = step - 4
        if 0 <= n < 16:
            q4 = n % 4
            cbase = 288 + q4 * 8
            S.wait("scalar", "dve", out_done_dve.get(n - 1, 0))
            apad(8)
            a_op("activation", ln2t[:], ffnbuf[n % 3][:], AF.Identity,
                 bias=vc(352 + q4), scale=1.0)
            ffa[n] = a_op("activation", ln3t[:], ln2t[:], AF.Identity,
                          bias=0.0, scale=vc(cbase + 4), inc=True)
            S.wait("vector", "act", ffa[n])
            S.wait("vector", "st", out_done.get(n - 2, 0))
            v_op("tensor_tensor", out=outst[n % 2][:], in0=ln3t[:],
                 in1=x2[:, n * KB:(n + 1) * KB], op=OP.add, inc=True)
            out_done_dve[n] = S.c["dve"]
            S.wait("gpsimd", "dve", S.c["dve"])
            out_done[n] = dma("gpsimd", "st", out_d[n * 128:(n + 1) * 128, :],
                              outst[n % 2][:])

    if DEBUG:
        S.wait("gpsimd", "dve", S.c["dve"])
        S.wait("gpsimd", "act", S.c["act"])
        S.wait("gpsimd", "mm", S.c["mm"])
        dma("gpsimd", "st", dbg_ot, OT[:])
        dma("gpsimd", "st", dbg_x2, x2[:])

    S.wait("gpsimd", "st", S.c["st"])

    with nc.Block() as block:
        @block.sync
        def _(e):
            for f in S.ops["sync"]:
                f(e)

        @block.gpsimd
        def _(e):
            for f in S.ops["gpsimd"]:
                f(e)

        @block.tensor
        def _(e):
            for f in S.ops["tensor"]:
                f(e)

        @block.vector
        def _(e):
            for f in S.ops["vector"]:
                f(e)

        @block.scalar
        def _(e):
            for f in S.ops["scalar"]:
                f(e)

    for g in reversed(sem_stack):
        g.__exit__(None, None, None)
    for g in reversed(psum_stack):
        g.__exit__(None, None, None)
    arena.__exit__(None, None, None)
    return nc




def kernel(L0_attn_hidden_states, L0_attn_Wq, L0_attn_Wk, L0_attn_Wv,
           L0_attn_Wo, L0_attn_rel_bias_table, L0_ffn_rhs):
    x = np.ascontiguousarray(np.asarray(L0_attn_hidden_states, dtype=np.float32))
    wq = np.ascontiguousarray(np.asarray(L0_attn_Wq, dtype=np.float32))
    wk = np.ascontiguousarray(np.asarray(L0_attn_Wk, dtype=np.float32))
    wv = np.ascontiguousarray(np.asarray(L0_attn_Wv, dtype=np.float32))
    wo = np.ascontiguousarray(np.asarray(L0_attn_Wo, dtype=np.float32))
    tbl = np.ascontiguousarray(np.asarray(L0_attn_rel_bias_table, dtype=np.float32))
    rhs = np.ascontiguousarray(np.asarray(L0_ffn_rhs, dtype=np.float32))

    iden = np.eye(128, dtype=np.float32)
    onescol = np.zeros((128, 64), np.float32)
    onescol[:, 0] = 1.0
    onesrow = np.ones((128, 128), np.float32)

    geo = {p: _host_geometry(p) for p in (0, 1)}

    nc = build_nc()
    in_maps = []
    for core in range(8):
        b, par = core // 2, core % 2
        xr = np.roll(x[b], -2048 * par, axis=0) if par else x[b]
        oh, sl, sh = geo[par]
        in_maps.append({
            "x": np.ascontiguousarray(xr),
            "wq": wq, "wk": wk, "wv": wv, "wo": wo, "rhs": rhs,
            "tbl": tbl, "ohall": oh, "sel_lo": sl, "sel_hi": sh,
            "iden": iden, "onescol": onescol, "onesrow": onesrow,
        })

    res = run_bass_kernel_spmd(nc, in_maps, core_ids=list(range(8)))
    if DEBUG:
        global _dbg
        _dbg = res.results

    out = np.empty((4, SQ, DML), np.float32)
    for core in range(8):
        b, par = core // 2, core % 2
        out[b, par * NQ:(par + 1) * NQ, :] = res.results[core]["out"]
    return out
